# version 01
# baseline (speedup 1.0000x reference)
import numpy as np
import concourse.bass as bass
import concourse.mybir as mybir
from concourse.bass_utils import run_bass_kernel_spmd
from concourse.alu_op_type import AluOpType as OP
import contextlib

F32 = mybir.dt.float32
BF16 = mybir.dt.bfloat16
I32 = mybir.dt.int32
AF = mybir.ActivationFunctionType
AX = mybir.AxisListType

C = 8
T = 2048
D = 1024
NT = 16
NC4 = 4
H = 16
DH = 64
M = 256
DN = DH ** -0.25
RATIO = M ** -0.5
PEPS = 1e-4
LNEPS = 1e-5

_stop_after = None
DEBUG = {}


class Prog:

    def __init__(self):
        self.ops = []

    def _e(self, en, fn, inc=1, wait=True):
        self.ops.append((en, fn, inc, wait))

    def t(self, fn, inc=1, wait=True):
        self._e("tensor", fn, inc, wait)

    def v(self, fn):
        self._e("vector", fn)

    def s(self, fn):
        self._e("scalar", fn)

    def d(self, fn):
        self._e("sync", fn, 16)

    def mmchain(self, mms):
        n = len(mms)
        for i, fn in enumerate(mms):
            self.t(fn, inc=(1 if i == n - 1 else 0), wait=(i == 0))

    def install(self, block, sem):
        waits = []
        cum = 0
        for (en, fn, inc, wait) in self.ops:
            waits.append(cum)
            cum += inc
        total = cum
        ops = self.ops

        def mk(name, eng):
            for i, (en, fn, inc, wait) in enumerate(ops):
                if en != name:
                    continue
                if wait:
                    eng.wait_ge(sem, waits[i])
                inst = fn(eng)
                if inc:
                    inst.then_inc(sem, inc)
            if name == "sync":
                eng.wait_ge(sem, total)

        block.sync(lambda e: mk("sync", e))
        block.tensor(lambda e: mk("tensor", e))
        block.vector(lambda e: mk("vector", e))
        block.scalar(lambda e: mk("scalar", e))


def consts_for_core(c):
    import ml_dtypes
    eyef = np.eye(128, dtype=np.float32)
    eyeb = eyef.astype(ml_dtypes.bfloat16)
    onesf = np.ones((128, 512), np.float32)
    iota = ((c % 4) * T + np.arange(T, dtype=np.float32)).reshape(NT, 128).T.copy()
    aux = np.zeros((64, 128), np.float32)
    aux[:, 0:64] = 1e-6 * np.eye(64, dtype=np.float32)
    aux[:, 64:128] = 2.0 * np.eye(64, dtype=np.float32)
    return {"eyef": eyef, "eyeb": eyeb, "onesf": onesf, "iota": iota, "aux64": aux}


def decl_consts(nc):
    r = {}
    r["eyef"] = nc.dram_tensor("eyef", [128, 128], F32, kind="ExternalInput").ap()
    r["eyeb"] = nc.dram_tensor("eyeb", [128, 128], BF16, kind="ExternalInput").ap()
    r["onesf"] = nc.dram_tensor("onesf", [128, 512], F32, kind="ExternalInput").ap()
    r["iota"] = nc.dram_tensor("iota", [128, NT], F32, kind="ExternalInput").ap()
    r["aux64"] = nc.dram_tensor("aux64", [64, 128], F32, kind="ExternalInput").ap()
    return r


def load_consts(p, st, cd):
    for n in st:
        p.d(lambda e, n=n: e.dma_start(out=st[n][:, :], in_=cd[n]))


def alloc_consts(stack, nc, names):
    st = {}
    shapes = {"eyef": ([128, 128], F32), "eyeb": ([128, 128], BF16),
              "onesf": ([128, 512], F32), "iota": ([128, NT], F32),
              "aux64": ([64, 128], F32)}
    for n in names:
        sh, dt = shapes[n]
        st[n] = stack.enter_context(nc.sbuf_tensor("c_" + n, sh, dt))
    return st


def emit_load_xT(p, x_ap, xT, xTb=None):
    xr = x_ap.rearrange("(a p) m -> p a m", p=128)
    for j in range(8):
        p.d(lambda e, j=j: e.dma_start(out=xT[:, j * T:(j + 1) * T], in_=xr[:, j, :]))
        if xTb is not None:
            p.s(lambda e, j=j: e.copy(out=xTb[:, j * T:(j + 1) * T], in_=xT[:, j * T:(j + 1) * T]))


def emit_wload(p, W_ap, stage, dst_bf, nch=8, X=1024):
    wr = W_ap.rearrange("(a p) m -> p a m", p=128)
    for j in range(nch):
        p.d(lambda e, j=j: e.dma_start(out=stage[:, 0:X], in_=wr[:, j, :]))
        p.s(lambda e, j=j: e.copy(out=dst_bf[:, j * X:(j + 1) * X], in_=stage[:, 0:X]))


def emit_bT(p, b_ap, tmp8, ps, dst, st):
    p.d(lambda e: e.dma_start(out=tmp8[:, :], in_=b_ap.rearrange("(a m) -> a m", a=8)))
    p.t(lambda e: e.transpose(ps[0:128, 0:8], tmp8[:, :], st["eyef"][0:8, 0:8]))
    p.s(lambda e: e.copy(out=dst[:, :], in_=ps[0:128, 0:8]))


def emit_headprojT(p, Wsb, xTb, h, ps, dst, scale=None, bias=None, M_=64):
    for tc in range(NC4):
        mms = []
        for j in range(8):
            mms.append(lambda e, j=j, tc=tc: e.matmul(
                ps[0:M_, 0:512], Wsb[:, j * D + h * 64:j * D + h * 64 + M_],
                xTb[:, j * T + tc * 512:j * T + tc * 512 + 512],
                start=(j == 0), stop=(j == 7)))
        p.mmchain(mms)
        if bias is not None:
            p.v(lambda e, tc=tc: e.tensor_scalar(
                out=dst[0:M_, tc * 512:tc * 512 + 512], in0=ps[0:M_, 0:512],
                scalar1=bias, scalar2=None, op0=OP.add))
        else:
            kw = {} if scale is None else {"scale": scale}
            p.s(lambda e, tc=tc, kw=kw: e.activation(
                out=dst[0:M_, tc * 512:tc * 512 + 512], in_=ps[0:M_, 0:512], func=AF.Copy, **kw))


def emit_sqrow(p, src, sq, ps, dst_row, st, scale=1.0, M_=64, width=T):
    nc_ = (width + 511) // 512
    for tc in range(nc_):
        w = min(512, width - tc * 512)
        p.s(lambda e, tc=tc, w=w: e.activation(out=sq[0:M_, 0:w],
                                               in_=src[0:M_, tc * 512:tc * 512 + w], func=AF.Square))
        p.t(lambda e, tc=tc, w=w: e.matmul(ps[0:1, 0:w], st["onesf"][0:M_, 0:1],
                                           sq[0:M_, 0:w], start=True, stop=True))
        p.s(lambda e, tc=tc, w=w: e.activation(out=dst_row[0:1, tc * 512:tc * 512 + w],
                                               in_=ps[0:1, 0:w], func=AF.Copy, scale=scale))


def emit_ln_inplace(p, xT, xTb, psS, psQ, psB, sq, rows, st, Tw=T, ncw=NC4):
    mean = rows[0:1, 0:512]
    msq = rows[0:1, 512:1024]
    scr = rows[0:1, 1024:1536]
    rstd = rows[0:1, 1536:2048]
    for tc in range(ncw):
        mmsS = []
        for j in range(8):
            mmsS.append(lambda e, j=j, tc=tc: e.matmul(
                psS[0:1, 0:512], st["onesf"][:, 0:1],
                xT[:, j * Tw + tc * 512:j * Tw + tc * 512 + 512],
                start=(j == 0), stop=(j == 7)))
        p.mmchain(mmsS)
        p.s(lambda e: e.activation(out=mean, in_=psS[0:1, 0:512], func=AF.Copy, scale=1.0 / D))
        for j in range(8):
            p.s(lambda e, j=j, tc=tc: e.activation(
                out=sq[:, 0:512], in_=xT[:, j * Tw + tc * 512:j * Tw + tc * 512 + 512],
                func=AF.Square))
            p.t(lambda e, j=j: e.matmul(psQ[0:1, 0:512], st["onesf"][:, 0:1], sq[:, 0:512],
                                        start=(j == 0), stop=(j == 7)), inc=1, wait=True)
        p.s(lambda e: e.activation(out=msq, in_=psQ[0:1, 0:512], func=AF.Copy, scale=1.0 / D))
        p.s(lambda e: e.activation(out=scr, in_=mean, func=AF.Square))
        p.v(lambda e: e.tensor_tensor(out=msq, in0=msq, in1=scr, op=OP.subtract))
        p.v(lambda e: e.tensor_scalar(out=msq, in0=msq, scalar1=LNEPS, scalar2=None, op0=OP.add))
        p.s(lambda e: e.activation(out=msq, in_=msq, func=AF.Sqrt))
        p.v(lambda e: e.reciprocal(rstd, msq))
        p.t(lambda e: e.matmul(psB[:, 0:512], st["onesf"][0:1, 0:128], mean, start=True, stop=True))
        p.s(lambda e: e.copy(out=sq[:, 0:512], in_=psB[:, 0:512]))
        p.t(lambda e: e.matmul(psB[:, 0:512], st["onesf"][0:1, 0:128], rstd, start=True, stop=True))
        p.s(lambda e: e.copy(out=sq[:, 512:1024], in_=psB[:, 0:512]))
        for j in range(8):
            sl = slice(j * Tw + tc * 512, j * Tw + tc * 512 + 512)
            p.v(lambda e, sl=sl: e.tensor_tensor(out=xT[:, sl], in0=xT[:, sl],
                                                 in1=sq[:, 0:512], op=OP.subtract))
            p.v(lambda e, sl=sl: e.tensor_tensor(out=xT[:, sl], in0=xT[:, sl],
                                                 in1=sq[:, 512:1024], op=OP.mult))
            if xTb is not None:
                p.s(lambda e, sl=sl: e.copy(out=xTb[:, sl], in_=xT[:, sl]))


def emit_store_xT(p, xo_ap, xT):
    xoR = xo_ap.rearrange("(a p) m -> p a m", p=128)
    for j in range(8):
        p.d(lambda e, j=j: e.dma_start(out=xoR[:, j, :], in_=xT[:, j * T:(j + 1) * T]))


def _mk_nc():
    return bass.Bass("TRN2", target_bir_lowering=False)


def build_L1():
    nc = _mk_nc()
    x = nc.dram_tensor("x", [D, T], F32, kind="ExternalInput").ap()
    Wk = nc.dram_tensor("Wk", [D, D], F32, kind="ExternalInput").ap()
    PT = nc.dram_tensor("PT", [64, 256], F32, kind="ExternalInput").ap()
    cd = decl_consts(nc)
    mxo = nc.dram_tensor("mx_part", [1, 1], F32, kind="ExternalOutput").ap()
    p = Prog()
    with contextlib.ExitStack() as stack:
        sb = lambda n, s, d: stack.enter_context(nc.sbuf_tensor(n, s, d))
        ps_ = lambda n, s, d: stack.enter_context(nc.psum_tensor(n, s, d))
        xT = sb("xT", [128, 8 * T], F32)
        xTb = sb("xTb", [128, 8 * T], BF16)
        Wsb = sb("Wsb", [128, 8 * D], BF16)
        stg = sb("stg", [128, D], F32)
        PTb = sb("PTb", [64, 256], BF16)
        kh = sb("kh", [64, T], BF16)
        gmax = sb("gmax", [128, 1], F32)
        rmax = sb("rmax", [128, 1], F32)
        grow = sb("grow", [1, 128], F32)
        gsc = sb("gsc", [1, 1], F32)
        st = alloc_consts(stack, nc, ["eyef", "onesf"])
        pA = ps_("pA", [128, 512], F32)
        pB = ps_("pB", [128, 512], F32)
        sem = stack.enter_context(nc.semaphore("S"))
        block = stack.enter_context(nc.Block())
        load_consts(p, st, cd)
        p.d(lambda e: e.dma_start(out=stg[0:64, 0:256], in_=PT))
        p.s(lambda e: e.copy(out=PTb[:, :], in_=stg[0:64, 0:256]))
        emit_load_xT(p, x, xT, xTb)
        emit_wload(p, Wk, stg, Wsb)
        p.v(lambda e: e.memset(gmax[:, :], -1e30))
        for h in range(H):
            emit_headprojT(p, Wsb, xTb, h, pA, kh, scale=DN)
            for t_ in range(NT):
                p.t(lambda e, t_=t_: e.matmul(pB[:, 0:256], kh[0:64, t_ * 128:t_ * 128 + 128],
                                              PTb[0:64, 0:256], start=True, stop=True))
                p.v(lambda e: e.reduce_max(rmax[:, :], pB[:, 0:256], axis=AX.X))
                p.v(lambda e: e.tensor_tensor(out=gmax[:, :], in0=gmax[:, :], in1=rmax[:, :], op=OP.max))
        p.t(lambda e: e.transpose(pA[0:1, 0:128], gmax[:, :], st["eyef"][:, :]))
        p.s(lambda e: e.copy(out=grow[:, :], in_=pA[0:1, 0:128]))
        p.v(lambda e: e.reduce_max(gsc[:, :], grow[:, :], axis=AX.X))
        p.d(lambda e: e.dma_start(out=mxo, in_=gsc[:, :]))
        p.install(block, sem)
    return nc


def build_L2():
    nc = _mk_nc()
    x = nc.dram_tensor("x", [D, T], F32, kind="ExternalInput").ap()
    Wk = nc.dram_tensor("Wk", [D, D], F32, kind="ExternalInput").ap()
    Wv = nc.dram_tensor("Wv", [D, D], F32, kind="ExternalInput").ap()
    PT = nc.dram_tensor("PT", [64, 256], F32, kind="ExternalInput").ap()
    mxa = nc.dram_tensor("mx_all", [8], F32, kind="ExternalInput").ap()
    cd = decl_consts(nc)
    ctxo = nc.dram_tensor("ctx_part", [128, 32 * 65], F32, kind="ExternalOutput").ap()
    p = Prog()
    with contextlib.ExitStack() as stack:
        sb = lambda n, s, d: stack.enter_context(nc.sbuf_tensor(n, s, d))
        ps_ = lambda n, s, d: stack.enter_context(nc.psum_tensor(n, s, d))
        xT = sb("xT", [128, 8 * T], F32)
        xTb = sb("xTb", [128, 8 * T], BF16)
        Wsb = sb("Wsb", [128, 8 * D], BF16)
        Vsb = sb("Vsb", [128, 8 * D], BF16)
        stg = sb("stg", [128, D], F32)
        PTb = sb("PTb", [64, 256], BF16)
        kh = sb("kh", [64, T], BF16)
        sq = sb("sq", [64, 512], F32)
        drow = sb("drow", [1, T], F32)
        vaug = sb("vaug", [128, NT * 16 * 65], BF16)
        mxs = sb("mxs", [1, 8], F32)
        negmx = sb("negmx", [128, 1], F32)
        ebias = sb("ebias", [128, 1], F32)
        kexp = sb("kexp", [128, 256], F32)
        kpb = sb("kpb", [128, 256], BF16)
        ctxs = sb("ctxs", [128, 32 * 65], F32)
        st = alloc_consts(stack, nc, ["onesf"])
        pA = ps_("pA", [128, 512], F32)
        pB = ps_("pB", [128, 512], F32)
        pC = ps_("pC", [128, 512], F32)
        pD = ps_("pD", [128, 512], F32)
        pE = ps_("pE", [128, 512], F32)
        sem = stack.enter_context(nc.semaphore("S"))
        block = stack.enter_context(nc.Block())
        load_consts(p, st, cd)
        p.d(lambda e: e.dma_start(out=stg[0:64, 0:256], in_=PT))
        p.s(lambda e: e.copy(out=PTb[:, :], in_=stg[0:64, 0:256]))
        p.d(lambda e: e.dma_start(out=mxs[:, :], in_=mxa.rearrange("(a m) -> a m", a=1)))
        emit_load_xT(p, x, xT, xTb)
        emit_wload(p, Wk, stg, Wsb)
        emit_wload(p, Wv, stg, Vsb)
        p.v(lambda e: e.reduce_max(negmx[0:1, 0:1], mxs[:, :], axis=AX.X))
        p.t(lambda e: e.matmul(pA[:, 0:1], st["onesf"][0:1, 0:128], negmx[0:1, 0:1], start=True, stop=True))
        p.s(lambda e: e.activation(out=negmx[:, :], in_=pA[:, 0:1], func=AF.Copy, scale=-1.0))
        vr = vaug[:, :].rearrange("p (t h c) -> p t h c", t=NT, h=16)
        p.v(lambda e, vr=vr: e.memset(vr[:, :, :, 64:65], 1.0))
        for t_ in range(NT):
            for ic in range(2):
                mms = []
                for j in range(8):
                    mms.append(lambda e, t_=t_, ic=ic, j=j: e.matmul(
                        pA[:, 0:512], xTb[:, j * T + t_ * 128:j * T + t_ * 128 + 128],
                        Vsb[:, j * D + ic * 512:j * D + ic * 512 + 512],
                        start=(j == 0), stop=(j == 7)))
                p.mmchain(mms)
                p.s(lambda e, t_=t_, ic=ic, vr=vr: e.copy(
                    out=vr[:, t_, ic * 8:ic * 8 + 8, 0:64],
                    in_=pA[:, 0:512].rearrange("p (h c) -> p h c", h=8)))
        for h in range(H):
            emit_headprojT(p, Wsb, xTb, h, pA, kh, scale=DN)
            emit_sqrow(p, kh, sq, pA, drow, st, scale=0.5)
            for t_ in range(NT):
                p.t(lambda e, t_=t_: e.matmul(pB[:, 0:256], kh[0:64, t_ * 128:t_ * 128 + 128],
                                              PTb[0:64, 0:256], start=True, stop=True))
                p.t(lambda e, t_=t_: e.matmul(pE[:, 0:1], drow[0:1, t_ * 128:t_ * 128 + 128],
                                              st["onesf"][0:1, 0:1], start=True, stop=True))
                p.v(lambda e: e.tensor_tensor(out=ebias[:, :], in0=negmx[:, :], in1=pE[:, 0:1], op=OP.subtract))
                p.s(lambda e: e.activation(out=kexp[:, :], in_=pB[:, 0:256], func=AF.Exp, bias=ebias[:, 0:1]))
                p.v(lambda e: e.tensor_scalar(out=kpb[:, :], in0=kexp[:, :], scalar1=RATIO,
                                              scalar2=RATIO * PEPS, op0=OP.mult, op1=OP.add))
                for blk, pp in ((0, pC), (1, pD)):
                    p.t(lambda e, t_=t_, blk=blk, pp=pp, h=h, vr=vr: e.matmul(
                        pp[:, 0:65], kpb[:, blk * 128:blk * 128 + 128], vr[:, t_, h, :],
                        start=(t_ == 0), stop=(t_ == NT - 1)), inc=1, wait=True)
            for blk, pp in ((0, pC), (1, pD)):
                p.s(lambda e, h=h, blk=blk, pp=pp: e.copy(
                    out=ctxs[:, (h * 2 + blk) * 65:(h * 2 + blk) * 65 + 65], in_=pp[:, 0:65]))
        p.d(lambda e: e.dma_start(out=ctxo, in_=ctxs[:, :]))
        p.install(block, sem)
    return nc


def build_L3():
    nc = _mk_nc()
    x = nc.dram_tensor("x", [D, T], F32, kind="ExternalInput").ap()
    Wq = nc.dram_tensor("Wq", [D, D], F32, kind="ExternalInput").ap()
    Wo = nc.dram_tensor("Wo", [D, D], F32, kind="ExternalInput").ap()
    bo = nc.dram_tensor("bo", [D], F32, kind="ExternalInput").ap()
    PT = nc.dram_tensor("PT", [64, 256], F32, kind="ExternalInput").ap()
    ctxa = nc.dram_tensor("ctx_own", [4, 128, 32 * 65], F32, kind="ExternalInput").ap()
    gw = nc.dram_tensor("gw", [D, 8], F32, kind="ExternalInput").ap()
    gb = nc.dram_tensor("gb", [8], F32, kind="ExternalInput").ap()
    cd = decl_consts(nc)
    xo = nc.dram_tensor("x1T", [D, T], F32, kind="ExternalOutput").ap()
    stato = nc.dram_tensor("stats", [1, 16], F32, kind="ExternalOutput").ap()
    p = Prog()
    with contextlib.ExitStack() as stack:
        sb = lambda n, s, d: stack.enter_context(nc.sbuf_tensor(n, s, d))
        ps_ = lambda n, s, d: stack.enter_context(nc.psum_tensor(n, s, d))
        xT = sb("xT", [128, 8 * T], F32)
        xTb = sb("xTb", [128, 8 * T], BF16)
        Wsb = sb("Wsb", [128, 8 * D], BF16)
        stg = sb("stg", [128, D], F32)
        PTb = sb("PTb", [64, 256], BF16)
        qh = sb("qh", [64, T], BF16)
        sq = sb("sq", [64, 512], F32)
        sq2 = sb("sq2", [128, 1024], F32)
        drow = sb("drow", [1, T], F32)
        ctxsum = sb("ctxsum", [128, 32 * 65], F32)
        ctx4 = sb("ctx4", [128, 1040], F32)
        ctxb = sb("ctxb", [128, 128], BF16)
        kcsb = sb("kcsb", [128, 256], F32)
        kcsrow = sb("kcsrow", [1, 256], F32)
        ebias = sb("ebias", [128, 1], F32)
        rmax = sb("rmax", [128, 1], F32)
        qp = sb("qp", [128, 256], F32)
        tmp = sb("tmp", [128, 256], F32)
        den = sb("den", [128, 1], F32)
        dinv = sb("dinv", [128, 1], F32)
        qpsb = sb("qpsb", [128, 256], BF16)
        qpsT = sb("qpsT", [128, 256], BF16)
        attnT = sb("attnT", [128, 8 * T], BF16)
        boT = sb("boT", [128, 8], F32)
        tmp8 = sb("tmp8", [8, 128], F32)
        rows = sb("rows", [1, 2048], F32)
        gsb = sb("gsb", [128, 64], F32)
        lg = sb("lg", [8, T], F32)
        gbT = sb("gbT", [8, 1], F32)
        negM = sb("negM", [8, 1], F32)
        S8 = sb("S8", [8, 1], F32)
        stat = sb("stat", [1, 16], F32)
        st = alloc_consts(stack, nc, ["eyef", "eyeb", "onesf"])
        pA = ps_("pA", [128, 512], F32)
        pB = ps_("pB", [128, 512], F32)
        pC = ps_("pC", [128, 512], F32)
        pD = ps_("pD", [128, 512], F32)
        pE = ps_("pE", [128, 512], F32)
        pT = ps_("pT", [128, 256], BF16)
        sem = stack.enter_context(nc.semaphore("S"))
        block = stack.enter_context(nc.Block())
        load_consts(p, st, cd)
        p.d(lambda e: e.dma_start(out=stg[0:64, 0:256], in_=PT))
        p.s(lambda e: e.copy(out=PTb[:, :], in_=stg[0:64, 0:256]))
        emit_load_xT(p, x, xT, xTb)
        emit_wload(p, Wq, stg, Wsb)
        for i in range(4):
            for hf in range(2):
                p.d(lambda e, i=i, hf=hf: e.dma_start(out=ctx4[:, :],
                                                      in_=ctxa[i, :, hf * 1040:hf * 1040 + 1040]))
                if i == 0:
                    p.v(lambda e, hf=hf: e.tensor_copy(
                        out=ctxsum[:, hf * 1040:hf * 1040 + 1040], in_=ctx4[:, :]))
                else:
                    p.v(lambda e, hf=hf: e.tensor_tensor(
                        out=ctxsum[:, hf * 1040:hf * 1040 + 1040],
                        in0=ctxsum[:, hf * 1040:hf * 1040 + 1040], in1=ctx4[:, :], op=OP.add))
        for h in range(H):
            emit_headprojT(p, Wsb, xTb, h, pA, qh, scale=DN)
            emit_sqrow(p, qh, sq, pA, drow, st, scale=0.5)
            for blk in range(2):
                p.t(lambda e, h=h, blk=blk: e.transpose(
                    pE[0:1, blk * 128:blk * 128 + 128],
                    ctxsum[:, (h * 2 + blk) * 65 + 64:(h * 2 + blk) * 65 + 65], st["eyef"][:, :]),
                    inc=(1 if blk == 1 else 0), wait=(blk == 0))
            p.s(lambda e: e.copy(out=kcsrow[:, :], in_=pE[0:1, 0:256]))
            p.t(lambda e: e.matmul(pE[:, 0:256], st["onesf"][0:1, 0:128], kcsrow[0:1, 0:256],
                                   start=True, stop=True))
            p.s(lambda e: e.copy(out=kcsb[:, :], in_=pE[:, 0:256]))
            for blk in range(2):
                p.s(lambda e, h=h, blk=blk: e.copy(
                    out=ctxb[:, blk * 64:blk * 64 + 64],
                    in_=ctxsum[:, (h * 2 + blk) * 65:(h * 2 + blk) * 65 + 64]))
            for t_ in range(NT):
                p.t(lambda e, t_=t_: e.matmul(pB[:, 0:256], qh[0:64, t_ * 128:t_ * 128 + 128],
                                              PTb[0:64, 0:256], start=True, stop=True))
                p.t(lambda e, t_=t_: e.matmul(pC[:, 0:1], drow[0:1, t_ * 128:t_ * 128 + 128],
                                              st["onesf"][0:1, 0:1], start=True, stop=True))
                p.v(lambda e: e.reduce_max(rmax[:, :], pB[:, 0:256], axis=AX.X, negate=True))
                p.v(lambda e: e.tensor_tensor(out=ebias[:, :], in0=rmax[:, :], in1=pC[:, 0:1], op=OP.subtract))
                p.s(lambda e: e.activation(out=qp[:, :], in_=pB[:, 0:256], func=AF.Exp, bias=ebias[:, 0:1]))
                p.v(lambda e: e.tensor_scalar(out=qp[:, :], in0=qp[:, :], scalar1=RATIO,
                                              scalar2=RATIO * PEPS, op0=OP.mult, op1=OP.add))
                p.v(lambda e: e.tensor_tensor(out=tmp[:, :], in0=qp[:, :], in1=kcsb[:, :], op=OP.mult))
                p.v(lambda e: e.reduce_sum(den[:, :], tmp[:, :], axis=AX.X))
                p.v(lambda e: e.reciprocal(dinv[:, :], den[:, :]))
                p.v(lambda e: e.tensor_scalar(out=qpsb[:, :], in0=qp[:, :], scalar1=dinv[:, 0:1],
                                              scalar2=None, op0=OP.mult))
                for blk in range(2):
                    p.t(lambda e, blk=blk: e.transpose(pT[:, blk * 128:blk * 128 + 128],
                                                       qpsb[:, blk * 128:blk * 128 + 128],
                                                       st["eyeb"][:, :]),
                        inc=(1 if blk == 1 else 0), wait=(blk == 0))
                p.s(lambda e: e.copy(out=qpsT[:, :], in_=pT[:, 0:256]))
                p.mmchain([lambda e, blk=blk: e.matmul(
                    pD[0:64, 0:128], ctxb[:, blk * 64:blk * 64 + 64],
                    qpsT[:, blk * 128:blk * 128 + 128], start=(blk == 0), stop=(blk == 1))
                    for blk in range(2)])
                p.s(lambda e, h=h, t_=t_: e.copy(
                    out=attnT[(h % 2) * 64:(h % 2) * 64 + 64,
                              (h // 2) * T + t_ * 128:(h // 2) * T + t_ * 128 + 128],
                    in_=pD[0:64, 0:128]))
        emit_wload(p, Wo, stg, Wsb)
        emit_bT(p, bo, tmp8, pA, boT, st)
        for ob in range(8):
            for tc in range(NC4):
                mms = []
                for j in range(8):
                    mms.append(lambda e, ob=ob, tc=tc, j=j: e.matmul(
                        pA[:, 0:512], Wsb[:, j * D + ob * 128:j * D + ob * 128 + 128],
                        attnT[:, j * T + tc * 512:j * T + tc * 512 + 512],
                        start=(j == 0), stop=(j == 7)))
                p.mmchain(mms)
                p.v(lambda e, ob=ob: e.tensor_scalar(
                    out=sq2[:, 0:512], in0=pA[:, 0:512], scalar1=boT[:, ob:ob + 1],
                    scalar2=None, op0=OP.add))
                p.v(lambda e, ob=ob, tc=tc: e.tensor_tensor(
                    out=xT[:, ob * T + tc * 512:ob * T + tc * 512 + 512],
                    in0=xT[:, ob * T + tc * 512:ob * T + tc * 512 + 512],
                    in1=sq2[:, 0:512], op=OP.add))
        emit_ln_inplace(p, xT, xTb, pB, pC, pD, sq2, rows, st)
        gr = gw.rearrange("(a p) m -> p a m", p=128)
        for j in range(8):
            p.d(lambda e, j=j: e.dma_start(out=gsb[:, j * 8:j * 8 + 8], in_=gr[:, j, :]))
        p.d(lambda e: e.dma_start(out=tmp8[0:1, 0:8], in_=gb.rearrange("(a m) -> a m", a=1)))
        p.t(lambda e: e.transpose(pA[0:8, 0:1], tmp8[0:1, 0:8], st["eyef"][0:1, 0:1]))
        p.s(lambda e: e.copy(out=gbT[:, :], in_=pA[0:8, 0:1]))
        for tc in range(NC4):
            mms = []
            for j in range(8):
                mms.append(lambda e, tc=tc, j=j: e.matmul(
                    pA[0:8, 0:512], gsb[:, j * 8:j * 8 + 8],
                    xT[:, j * T + tc * 512:j * T + tc * 512 + 512],
                    start=(j == 0), stop=(j == 7)))
            p.mmchain(mms)
            p.v(lambda e, tc=tc: e.tensor_scalar(out=lg[:, tc * 512:tc * 512 + 512],
                                                 in0=pA[0:8, 0:512], scalar1=gbT[:, 0:1],
                                                 scalar2=None, op0=OP.add))
        p.v(lambda e: e.reduce_max(negM[:, :], lg[:, :], axis=AX.X, negate=True))
        p.s(lambda e: e.activation(out=lg[:, :], in_=lg[:, :], func=AF.Exp, bias=negM[:, 0:1]))
        p.v(lambda e: e.reduce_sum(S8[:, :], lg[:, :], axis=AX.X))
        p.t(lambda e: e.transpose(pA[0:1, 0:8], negM[0:8, 0:1], st["eyef"][0:8, 0:8]))
        p.s(lambda e: e.activation(out=stat[0:1, 0:8], in_=pA[0:1, 0:8], func=AF.Copy, scale=-1.0))
        p.t(lambda e: e.transpose(pA[0:1, 8:16], S8[0:8, 0:1], st["eyef"][0:8, 0:8]))
        p.s(lambda e: e.copy(out=stat[0:1, 8:16], in_=pA[0:1, 8:16]))
        p.d(lambda e: e.dma_start(out=stato, in_=stat[:, :]))
        emit_store_xT(p, xo, xT)
        p.install(block, sem)
    return nc



def emit_l0gates(p, st, gw_ap, gb_ap, statown_ap, xsrc_f32_chunks, gsb, gbT, lg,
                 stats_sb, mg, sg, s4, prb, wtf, indf, wtb, eq8, mk8, m1c, m2c,
                 pG, pA, tmp8):
    gr = gw_ap.rearrange("(a p) m -> p a m", p=128)
    for j in range(8):
        p.d(lambda e, j=j: e.dma_start(out=gsb[:, j * 8:j * 8 + 8], in_=gr[:, j, :]))
    p.d(lambda e: e.dma_start(out=tmp8[0:1, 0:8], in_=gb_ap.rearrange("(a m) -> a m", a=1)))
    p.t(lambda e: e.transpose(pA[0:8, 0:1], tmp8[0:1, 0:8], st["eyef"][0:1, 0:1]))
    p.s(lambda e: e.copy(out=gbT[:, :], in_=pA[0:8, 0:1]))
    p.d(lambda e: e.dma_start(out=stats_sb[:, :],
                              in_=statown_ap.rearrange("(x a) b -> x (a b)", x=1)))
    for j in range(8):
        xc = xsrc_f32_chunks(j)
        for tc in range(NC4):
            p.t(lambda e, j=j, tc=tc, xc=xc: e.matmul(
                pG[tc][0:8, 0:512], gsb[:, j * 8:j * 8 + 8], xc[:, tc * 512:tc * 512 + 512],
                start=(j == 0), stop=(j == 7)), inc=1, wait=True)
    for tc in range(NC4):
        p.v(lambda e, tc=tc: e.tensor_scalar(out=lg[:, tc * 512:tc * 512 + 512],
                                             in0=pG[tc][0:8, 0:512], scalar1=gbT[:, 0:1],
                                             scalar2=None, op0=OP.add))
    sr = stats_sb[0:1, :].rearrange("p (c s) -> p s c", s=16)
    for e_ in range(8):
        p.v(lambda e, e_=e_: e.reduce_max(mg[0:1, e_:e_ + 1], sr[0:1, e_, :], axis=AX.X))
        p.v(lambda e, e_=e_: e.tensor_scalar(out=s4[0:1, 0:4], in0=sr[0:1, e_, :],
                                             scalar1=mg[0:1, e_:e_ + 1], scalar2=None,
                                             op0=OP.subtract))
        p.s(lambda e: e.activation(out=s4[0:1, 0:4], in_=s4[0:1, 0:4], func=AF.Exp))
        p.v(lambda e, e_=e_: e.tensor_tensor(out=s4[0:1, 4:8], in0=s4[0:1, 0:4],
                                             in1=sr[0:1, 8 + e_, :], op=OP.mult))
        p.v(lambda e, e_=e_: e.reduce_sum(sg[0:1, e_:e_ + 1], s4[0:1, 4:8], axis=AX.X))
    p.v(lambda e: e.tensor_scalar(out=s4[0:1, 8:16], in0=mg[0:1, 0:8], scalar1=-1.0,
                                  scalar2=None, op0=OP.mult))
    p.t(lambda e: e.transpose(pA[0:8, 0:1], s4[0:1, 8:16], st["eyef"][0:1, 0:1]))
    p.s(lambda e: e.copy(out=m1c[0:8, 0:1], in_=pA[0:8, 0:1]))
    p.v(lambda e: e.reciprocal(s4[0:1, 8:16], sg[0:1, 0:8]))
    p.t(lambda e: e.transpose(pA[0:8, 1:2], s4[0:1, 8:16], st["eyef"][0:1, 0:1]))
    p.s(lambda e: e.copy(out=m2c[0:8, 0:1], in_=pA[0:8, 1:2]))
    p.s(lambda e: e.activation(out=lg[:, :], in_=lg[:, :], func=AF.Exp, bias=m1c[0:8, 0:1]))
    p.v(lambda e: e.tensor_scalar(out=lg[:, :], in0=lg[:, :], scalar1=m2c[0:8, 0:1],
                                  scalar2=None, op0=OP.mult))
    for t_ in range(NT):
        p.t(lambda e, t_=t_: e.transpose(pA[0:128, 0:8], lg[0:8, t_ * 128:t_ * 128 + 128],
                                         st["eyef"][0:8, 0:8]))
        p.s(lambda e, t_=t_: e.copy(out=prb[:, t_ * 8:t_ * 8 + 8], in_=pA[0:128, 0:8]))
        pr = prb[:, t_ * 8:t_ * 8 + 8]
        p.v(lambda e, pr=pr: e.reduce_max(m1c[:, 0:1], pr, axis=AX.X))
        p.v(lambda e, pr=pr: e.tensor_scalar(out=eq8[:, :], in0=pr, scalar1=m1c[:, 0:1],
                                             scalar2=None, op0=OP.is_equal))
        p.v(lambda e, pr=pr: e.scalar_tensor_tensor(out=mk8[:, :], in0=eq8[:, :], scalar=-2.0,
                                                    in1=pr, op0=OP.mult, op1=OP.add))
        p.v(lambda e: e.reduce_max(m2c[:, 0:1], mk8[:, :], axis=AX.X))
        p.v(lambda e, pr=pr, t_=t_: e.tensor_scalar(out=indf[:, t_ * 8:t_ * 8 + 8], in0=pr,
                                                    scalar1=m2c[:, 0:1], scalar2=None,
                                                    op0=OP.is_ge))
        p.v(lambda e, pr=pr, t_=t_: e.tensor_tensor(out=wtf[:, t_ * 8:t_ * 8 + 8], in0=pr,
                                                    in1=indf[:, t_ * 8:t_ * 8 + 8], op=OP.mult))
        if wtb is not None:
            p.s(lambda e, t_=t_: e.copy(out=wtb[:, t_ * 8:t_ * 8 + 8],
                                        in_=wtf[:, t_ * 8:t_ * 8 + 8]))


def build_L4():
    nc = _mk_nc()
    x = nc.dram_tensor("x", [D, T], F32, kind="ExternalInput").ap()
    gw = nc.dram_tensor("gw", [D, 8], F32, kind="ExternalInput").ap()
    gb = nc.dram_tensor("gb", [8], F32, kind="ExternalInput").ap()
    statown = nc.dram_tensor("stats_own", [4, 16], F32, kind="ExternalInput").ap()
    w1 = nc.dram_tensor("w1", [8, D, D], F32, kind="ExternalInput").ap()
    b1 = nc.dram_tensor("b1", [8, D], F32, kind="ExternalInput").ap()
    w2 = nc.dram_tensor("w2", [8, D, D], F32, kind="ExternalInput").ap()
    b2 = nc.dram_tensor("b2", [8, D], F32, kind="ExternalInput").ap()
    w3 = nc.dram_tensor("w3", [8, D, D], F32, kind="ExternalInput").ap()
    b3 = nc.dram_tensor("b3", [8, D], F32, kind="ExternalInput").ap()
    cd = decl_consts(nc)
    vo = nc.dram_tensor("v_part", [1, 8192], F32, kind="ExternalOutput").ap()
    p = Prog()
    with contextlib.ExitStack() as stack:
        sb = lambda n, s, d: stack.enter_context(nc.sbuf_tensor(n, s, d))
        ps_ = lambda n, s, d: stack.enter_context(nc.psum_tensor(n, s, d))
        xTb = sb("xTb", [128, 8 * T], BF16)
        chunk = sb("chunk", [128, T], F32)
        stg = sb("stg", [128, D], F32)
        wsb = sb("wsb", [128, 8 * D], BF16)
        h1T = sb("h1T", [128, 8 * T], BF16)
        h2T = sb("h2T", [128, 8 * T], BF16)
        ytile = sb("ytile", [128, D], BF16)
        vacc = sb("vacc", [1, 8192], F32)
        gsb = sb("gsb", [128, 64], F32)
        gbT = sb("gbT", [8, 1], F32)
        lg = sb("lg", [8, T], F32)
        stats_sb = sb("stats_sb", [1, 64], F32)
        mg = sb("mg", [1, 8], F32)
        sg = sb("sg", [1, 8], F32)
        s4 = sb("s4", [1, 16], F32)
        prb = sb("prb", [128, NT * 8], F32)
        wtf = sb("wtf", [128, NT * 8], F32)
        indf = sb("indf", [128, NT * 8], F32)
        wtb = sb("wtb", [128, NT * 8], BF16)
        eq8 = sb("eq8", [128, 8], F32)
        mk8 = sb("mk8", [128, 8], F32)
        m1c = sb("m1c", [128, 1], F32)
        m2c = sb("m2c", [128, 1], F32)
        bcol = sb("bcol", [128, 8], F32)
        b3row = sb("b3row", [1, D], F32)
        tmp8 = sb("tmp8", [8, 128], F32)
        sq2 = sb("sq2", [128, 512], F32)
        st = alloc_consts(stack, nc, ["eyef", "onesf"])
        pA = ps_("pA", [128, 512], F32)
        pG = [ps_("pG%d" % i, [8, 512], F32) for i in range(4)]
        pV0 = ps_("pV0", [1, 512], F32)
        pV1 = ps_("pV1", [1, 512], F32)
        sem = stack.enter_context(nc.semaphore("S"))
        block = stack.enter_context(nc.Block())
        load_consts(p, st, cd)
        xr = x.rearrange("(a p) m -> p a m", p=128)

        def xchunk(j):
            p.d(lambda e, j=j: e.dma_start(out=chunk[:, :], in_=xr[:, j, :]))
            p.s(lambda e, j=j: e.copy(out=xTb[:, j * T:(j + 1) * T], in_=chunk[:, :]))
            return chunk
        emit_l0gates(p, st, gw, gb, statown, xchunk, gsb, gbT, lg, stats_sb, mg, sg,
                     s4, prb, wtf, indf, wtb, eq8, mk8, m1c, m2c, pG, pA, tmp8)
        for e_ in range(8):
            emit_wload(p, w1[e_, :, :], stg, wsb)
            emit_bT(p, b1[e_, :], tmp8, pA, bcol, st)
            for src, dst, wap, bap in ((xTb, h1T, None, None),):
                pass
            for ob in range(8):
                for tc in range(NC4):
                    mms = []
                    for j in range(8):
                        mms.append(lambda e, ob=ob, tc=tc, j=j: e.matmul(
                            pA[:, 0:512], wsb[:, j * D + ob * 128:j * D + ob * 128 + 128],
                            xTb[:, j * T + tc * 512:j * T + tc * 512 + 512],
                            start=(j == 0), stop=(j == 7)))
                    p.mmchain(mms)
                    p.v(lambda e, ob=ob, tc=tc: e.tensor_scalar(
                        out=h1T[:, ob * T + tc * 512:ob * T + tc * 512 + 512],
                        in0=pA[:, 0:512], scalar1=bcol[:, ob:ob + 1], scalar2=0.0,
                        op0=OP.add, op1=OP.max))
            emit_wload(p, w2[e_, :, :], stg, wsb)
            emit_bT(p, b2[e_, :], tmp8, pA, bcol, st)
            for ob in range(8):
                for tc in range(NC4):
                    mms = []
                    for j in range(8):
                        mms.append(lambda e, ob=ob, tc=tc, j=j: e.matmul(
                            pA[:, 0:512], wsb[:, j * D + ob * 128:j * D + ob * 128 + 128],
                            h1T[:, j * T + tc * 512:j * T + tc * 512 + 512],
                            start=(j == 0), stop=(j == 7)))
                    p.mmchain(mms)
                    p.v(lambda e, ob=ob, tc=tc: e.tensor_scalar(
                        out=h2T[:, ob * T + tc * 512:ob * T + tc * 512 + 512],
                        in0=pA[:, 0:512], scalar1=bcol[:, ob:ob + 1], scalar2=0.0,
                        op0=OP.add, op1=OP.max))
            emit_wload(p, w3[e_, :, :], stg, wsb)
            p.d(lambda e, e_=e_: e.dma_start(out=b3row[0:1, :],
                                             in_=b3[e_, :].rearrange("(a m) -> a m", a=1)))
            for t_ in range(NT):
                for oc in range(2):
                    mms = []
                    for j in range(8):
                        mms.append(lambda e, t_=t_, oc=oc, j=j: e.matmul(
                            pA[:, 0:512], h2T[:, j * T + t_ * 128:j * T + t_ * 128 + 128],
                            wsb[:, j * D + oc * 512:j * D + oc * 512 + 512],
                            start=(j == 0), stop=False))
                    mms.append(lambda e, oc=oc: e.matmul(
                        pA[:, 0:512], st["onesf"][0:1, 0:128], b3row[0:1, oc * 512:oc * 512 + 512],
                        start=False, stop=True))
                    p.mmchain(mms)
                    p.v(lambda e, oc=oc: e.tensor_scalar(
                        out=ytile[:, oc * 512:oc * 512 + 512], in0=pA[:, 0:512],
                        scalar1=0.0, scalar2=None, op0=OP.max))
                for oc, pv in ((0, pV0), (1, pV1)):
                    p.t(lambda e, t_=t_, e_=e_, oc=oc, pv=pv: e.matmul(
                        pv[0:1, 0:512], wtb[:, t_ * 8 + e_:t_ * 8 + e_ + 1],
                        ytile[:, oc * 512:oc * 512 + 512],
                        start=(t_ == 0), stop=(t_ == NT - 1)), inc=1, wait=True)
            for oc, pv in ((0, pV0), (1, pV1)):
                p.s(lambda e, e_=e_, oc=oc, pv=pv: e.copy(
                    out=vacc[0:1, e_ * 1024 + oc * 512:e_ * 1024 + oc * 512 + 512],
                    in_=pv[0:1, 0:512]))
        p.d(lambda e: e.dma_start(out=vo, in_=vacc[:, :]))
        p.install(block, sem)
    return nc


def build_L5():
    nc = _mk_nc()
    x = nc.dram_tensor("x", [D, T], F32, kind="ExternalInput").ap()
    gw = nc.dram_tensor("gw", [D, 8], F32, kind="ExternalInput").ap()
    gb = nc.dram_tensor("gb", [8], F32, kind="ExternalInput").ap()
    statown = nc.dram_tensor("stats_own", [4, 16], F32, kind="ExternalInput").ap()
    vown = nc.dram_tensor("v_own", [8, 8192], F32, kind="ExternalInput").ap()
    cd = decl_consts(nc)
    xo = nc.dram_tensor("x2T", [D, T], F32, kind="ExternalOutput").ap()
    vscr = nc.dram_tensor("vscr", [1, 8192], F32).ap()
    p = Prog()
    with contextlib.ExitStack() as stack:
        sb = lambda n, s, d: stack.enter_context(nc.sbuf_tensor(n, s, d))
        ps_ = lambda n, s, d: stack.enter_context(nc.psum_tensor(n, s, d))
        xT = sb("xT", [128, 8 * T], F32)
        gsb = sb("gsb", [128, 64], F32)
        gbT = sb("gbT", [8, 1], F32)
        lg = sb("lg", [8, T], F32)
        stats_sb = sb("stats_sb", [1, 64], F32)
        mg = sb("mg", [1, 8], F32)
        sg = sb("sg", [1, 8], F32)
        s4 = sb("s4", [1, 16], F32)
        prb = sb("prb", [128, NT * 8], F32)
        wtf = sb("wtf", [128, NT * 8], F32)
        indf = sb("indf", [128, NT * 8], F32)
        eq8 = sb("eq8", [128, 8], F32)
        mk8 = sb("mk8", [128, 8], F32)
        m1c = sb("m1c", [128, 1], F32)
        m2c = sb("m2c", [128, 1], F32)
        v4 = sb("v4", [1, 8192], F32)
        vsum = sb("vsum", [1, 8192], F32)
        v8b = sb("v8b", [8, 1024], BF16)
        v8f = sb("v8f", [8, 1024], F32)
        indT = sb("indT", [8, T], BF16)
        tmp8 = sb("tmp8", [8, 128], F32)
        sq2 = sb("sq2", [128, 1024], F32)
        rows = sb("rows", [1, 2048], F32)
        st = alloc_consts(stack, nc, ["eyef", "onesf"])
        pA = ps_("pA", [128, 512], F32)
        pB = ps_("pB", [128, 512], F32)
        pC = ps_("pC", [128, 512], F32)
        pD = ps_("pD", [128, 512], F32)
        pG = [ps_("pG%d" % i, [8, 512], F32) for i in range(4)]
        sem = stack.enter_context(nc.semaphore("S"))
        block = stack.enter_context(nc.Block())
        load_consts(p, st, cd)
        emit_load_xT(p, x, xT, None)

        def xchunk(j):
            return xT[:, j * T:(j + 1) * T]
        emit_l0gates(p, st, gw, gb, statown, xchunk, gsb, gbT, lg, stats_sb, mg, sg,
                     s4, prb, wtf, indf, None, eq8, mk8, m1c, m2c, pG, pA, tmp8)
        for i in range(8):
            p.d(lambda e, i=i: e.dma_start(out=v4[:, :], in_=vown[i:i + 1, :]))
            if i == 0:
                p.v(lambda e: e.tensor_copy(out=vsum[:, :], in_=v4[:, :]))
            else:
                p.v(lambda e: e.tensor_tensor(out=vsum[:, :], in0=vsum[:, :], in1=v4[:, :], op=OP.add))
        p.d(lambda e: e.dma_start(out=vscr, in_=vsum[:, :]))
        p.d(lambda e: e.dma_start(out=v8f[:, :], in_=vscr.rearrange("x (a b) -> (x a) b", a=8)))
        p.s(lambda e: e.copy(out=v8b[:, :], in_=v8f[:, :]))
        for t_ in range(NT):
            p.t(lambda e, t_=t_: e.transpose(pA[0:8, 0:128], indf[:, t_ * 8:t_ * 8 + 8],
                                             st["eyef"][:, :]))
            p.s(lambda e, t_=t_: e.copy(out=indT[0:8, t_ * 128:t_ * 128 + 128], in_=pA[0:8, 0:128]))
        for ob in range(8):
            for tc in range(NC4):
                p.t(lambda e, ob=ob, tc=tc: e.matmul(
                    pA[:, 0:512], v8b[0:8, ob * 128:ob * 128 + 128],
                    indT[0:8, tc * 512:tc * 512 + 512], start=True, stop=True))
                p.v(lambda e, ob=ob, tc=tc: e.tensor_tensor(
                    out=xT[:, ob * T + tc * 512:ob * T + tc * 512 + 512],
                    in0=xT[:, ob * T + tc * 512:ob * T + tc * 512 + 512],
                    in1=pA[:, 0:512], op=OP.add))
        emit_ln_inplace(p, xT, None, pB, pC, pD, sq2, rows, st)
        emit_store_xT(p, xo, xT)
        p.install(block, sem)
    return nc


def build_L6():
    nc = _mk_nc()
    x = nc.dram_tensor("x", [D, T], F32, kind="ExternalInput").ap()
    lidx = nc.dram_tensor("lidx", [64], I32, kind="ExternalInput").ap()
    cd = decl_consts(nc)
    xgo = nc.dram_tensor("xg_part", [64, D], F32, kind="ExternalOutput").ap()
    p = Prog()
    with contextlib.ExitStack() as stack:
        sb = lambda n, s, d: stack.enter_context(nc.sbuf_tensor(n, s, d))
        ps_ = lambda n, s, d: stack.enter_context(nc.psum_tensor(n, s, d))
        xT = sb("xT", [128, 8 * T], F32)
        xtok = sb("xtok", [128, D], F32)
        idxi = sb("idxi", [1, 64], I32)
        idxf = sb("idxf", [1, 64], F32)
        idxb = sb("idxb", [128, 64], F32)
        dif = sb("dif", [128, 64], F32)
        stile = sb("stile", [128, 64], F32)
        xgs = sb("xgs", [64, D], F32)
        st = alloc_consts(stack, nc, ["eyef", "onesf", "iota"])
        pA = ps_("pA", [128, 512], F32)
        pX0 = ps_("pX0", [64, 512], F32)
        pX1 = ps_("pX1", [64, 512], F32)
        sem = stack.enter_context(nc.semaphore("S"))
        block = stack.enter_context(nc.Block())
        load_consts(p, st, cd)
        emit_load_xT(p, x, xT, None)
        p.d(lambda e: e.dma_start(out=idxi[:, :], in_=lidx.rearrange("(a m) -> a m", a=1)))
        p.v(lambda e: e.tensor_copy(out=idxf[:, :], in_=idxi[:, :]))
        p.t(lambda e: e.matmul(pA[:, 0:64], st["onesf"][0:1, 0:128], idxf[0:1, 0:64],
                               start=True, stop=True))
        p.s(lambda e: e.copy(out=idxb[:, :], in_=pA[:, 0:64]))
        for t_ in range(NT):
            for j in range(8):
                p.t(lambda e, t_=t_, j=j: e.transpose(
                    pA[0:128, 0:128], xT[:, j * T + t_ * 128:j * T + t_ * 128 + 128],
                    st["eyef"][:, :]))
                p.s(lambda e, j=j: e.copy(out=xtok[:, j * 128:j * 128 + 128], in_=pA[0:128, 0:128]))
            p.v(lambda e, t_=t_: e.tensor_scalar(out=dif[:, :], in0=idxb[:, :],
                                                 scalar1=st["iota"][:, t_:t_ + 1],
                                                 scalar2=None, op0=OP.subtract))
            p.s(lambda e: e.activation(out=dif[:, :], in_=dif[:, :], func=AF.Square))
            p.v(lambda e: e.tensor_scalar(out=stile[:, :], in0=dif[:, :], scalar1=-1.0,
                                          scalar2=1.0, op0=OP.mult, op1=OP.add))
            p.v(lambda e: e.tensor_scalar(out=stile[:, :], in0=stile[:, :], scalar1=0.0,
                                          scalar2=None, op0=OP.max))
            for half, px in ((0, pX0), (1, pX1)):
                p.t(lambda e, t_=t_, half=half, px=px: e.matmul(
                    px[0:64, 0:512], stile[:, 0:64], xtok[:, half * 512:half * 512 + 512],
                    start=(t_ == 0), stop=(t_ == NT - 1)), inc=1, wait=True)
        for half, px in ((0, pX0), (1, pX1)):
            p.s(lambda e, half=half, px=px: e.copy(out=xgs[0:64, half * 512:half * 512 + 512],
                                                   in_=px[0:64, 0:512]))
        p.d(lambda e: e.dma_start(out=xgo, in_=xgs[:, :]))
        p.install(block, sem)
    return nc


def emit_xg_lm(p, st, xgown_ap, WkTsb, bkT, xg4, xgf, xgTb, lmT, lmb, lmn2b, lnrow,
               sq64, pA, pE):
    for i in range(4):
        for hf in range(2):
            p.d(lambda e, i=i, hf=hf: e.dma_start(out=xg4[0:64, 0:512],
                                                  in_=xgown_ap[i, :, hf * 512:hf * 512 + 512]))
            dst = lambda hf=hf: xgf[0:64, hf * 512:hf * 512 + 512]
            if i == 0:
                p.v(lambda e, hf=hf: e.tensor_copy(out=xgf[0:64, hf * 512:hf * 512 + 512],
                                                   in_=xg4[0:64, 0:512]))
            else:
                p.v(lambda e, hf=hf: e.tensor_tensor(out=xgf[0:64, hf * 512:hf * 512 + 512],
                                                     in0=xgf[0:64, hf * 512:hf * 512 + 512],
                                                     in1=xg4[0:64, 0:512], op=OP.add))
    for j in range(8):
        p.t(lambda e, j=j: e.transpose(pA[0:128, 0:64], xgf[0:64, j * 128:j * 128 + 128],
                                       st["eyef"][0:64, 0:64]))
        p.s(lambda e, j=j: e.copy(out=xgTb[:, j * 64:j * 64 + 64], in_=pA[0:128, 0:64]))
    for ib in range(8):
        mms = []
        for j in range(8):
            mms.append(lambda e, ib=ib, j=j: e.matmul(
                pA[0:128, 0:64], WkTsb[:, j * D + ib * 128:j * D + ib * 128 + 128],
                xgTb[:, j * 64:j * 64 + 64], start=(j == 0), stop=(j == 7)))
        p.mmchain(mms)
        p.v(lambda e, ib=ib: e.tensor_scalar(out=lmT[:, ib * 64:ib * 64 + 64],
                                             in0=pA[0:128, 0:64], scalar1=bkT[:, ib:ib + 1],
                                             scalar2=None, op0=OP.add))
    p.s(lambda e: e.copy(out=lmb[:, :], in_=lmT[:, :]))
    p.s(lambda e: e.activation(out=lmn2b[:, :], in_=lmT[:, :], func=AF.Copy, scale=-2.0))
    for h in range(H):
        sl_r = slice((h % 2) * 64, (h % 2) * 64 + 64)
        sl_c = slice((h // 2) * 64, (h // 2) * 64 + 64)
        p.s(lambda e, sl_r=sl_r, sl_c=sl_c: e.activation(out=sq64[0:64, 0:64],
                                                         in_=lmT[sl_r, sl_c], func=AF.Square))
        p.t(lambda e, h=h: e.matmul(pE[0:1, 0:64], st["onesf"][0:64, 0:1], sq64[0:64, 0:64],
                                    start=True, stop=True))
        p.s(lambda e, h=h: e.copy(out=lnrow[0:1, h * 64:h * 64 + 64], in_=pE[0:1, 0:64]))


def build_L7():
    nc = _mk_nc()
    x = nc.dram_tensor("x", [D, T], F32, kind="ExternalInput").ap()
    WkT = nc.dram_tensor("WkT", [D, D], F32, kind="ExternalInput").ap()
    bk = nc.dram_tensor("bk", [D], F32, kind="ExternalInput").ap()
    WvT = nc.dram_tensor("WvT", [D, D], F32, kind="ExternalInput").ap()
    bv = nc.dram_tensor("bv", [D], F32, kind="ExternalInput").ap()
    temp = nc.dram_tensor("temp", [1, 1], F32, kind="ExternalInput").ap()
    xgown = nc.dram_tensor("xg_own", [4, 64, D], F32, kind="ExternalInput").ap()
    cd = decl_consts(nc)
    zo = nc.dram_tensor("Z_part", [D, 65], F32, kind="ExternalOutput").ap()
    p = Prog()
    with contextlib.ExitStack() as stack:
        sb = lambda n, s, d: stack.enter_context(nc.sbuf_tensor(n, s, d))
        ps_ = lambda n, s, d: stack.enter_context(nc.psum_tensor(n, s, d))
        xT = sb("xT", [128, 8 * T], F32)
        xTb = sb("xTb", [128, 8 * T], BF16)
        WkTsb = sb("WkTsb", [128, 8 * D], BF16)
        WvTsb = sb("WvTsb", [128, 8 * D], BF16)
        stg = sb("stg", [128, D], F32)
        xg4 = sb("xg4", [64, 512], F32)
        xgf = sb("xgf", [64, D], F32)
        xgTb = sb("xgTb", [128, 8 * 64], BF16)
        lmT = sb("lmT", [128, 8 * 64], F32)
        lmb = sb("lmb", [128, 8 * 64], BF16)
        lmn2b = sb("lmn2b", [128, 8 * 64], BF16)
        lnrow = sb("lnrow", [1, H * 64], F32)
        sq64 = sb("sq64", [64, 64], F32)
        bkT = sb("bkT", [128, 8], F32)
        bvrow = sb("bvrow", [1, D], F32)
        kh = sb("kh", [64, T], BF16)
        sq = sb("sq", [64, 512], F32)
        knrow = sb("knrow", [1, T], F32)
        vaugh = sb("vaugh", [128, NT * 65], BF16)
        phk = sb("phk", [128, 64], BF16)
        lmn2h = sb("lmn2h", [64, 64], BF16)
        ck = sb("ck", [128, 64], F32)
        nit = sb("nit", [128, 1], F32)
        bcol = sb("bcol", [64, 1], F32)
        Zs = sb("Zs", [128, 8 * 65], F32)
        tmp8 = sb("tmp8", [8, 128], F32)
        st = alloc_consts(stack, nc, ["eyef", "onesf"])
        pA = ps_("pA", [128, 512], F32)
        pC = ps_("pC", [128, 512], F32)
        pZ = ps_("pZ", [128, 512], F32)
        pE = ps_("pE", [128, 512], F32)
        sem = stack.enter_context(nc.semaphore("S"))
        block = stack.enter_context(nc.Block())
        load_consts(p, st, cd)
        emit_load_xT(p, x, xT, xTb)
        emit_wload(p, WkT, stg, WkTsb)
        emit_wload(p, WvT, stg, WvTsb)
        emit_bT(p, bk, tmp8, pA, bkT, st)
        p.d(lambda e: e.dma_start(out=bvrow[0:1, :], in_=bv.rearrange("(a m) -> a m", a=1)))
        p.d(lambda e: e.dma_start(out=nit[0:1, 0:1], in_=temp))
        p.v(lambda e: e.reciprocal(nit[0:1, 0:1], nit[0:1, 0:1]))
        p.t(lambda e: e.matmul(pA[:, 0:1], st["onesf"][0:1, 0:128], nit[0:1, 0:1],
                               start=True, stop=True))
        p.s(lambda e: e.activation(out=nit[:, :], in_=pA[:, 0:1], func=AF.Copy, scale=-1.0))
        emit_xg_lm(p, st, xgown, WkTsb, bkT, xg4, xgf, xgTb, lmT, lmb, lmn2b, lnrow, sq64, pA, pE)
        vr = vaugh[:, :].rearrange("p (t c) -> p t c", c=65)
        p.v(lambda e, vr=vr: e.memset(vr[:, :, 64:65], 1.0))
        for h in range(H):
            p.s(lambda e, h=h: e.copy(out=bcol[0:64, 0:1],
                                      in_=bkT[(h % 2) * 64:(h % 2) * 64 + 64, (h // 2):(h // 2) + 1]))
            emit_headprojT(p, WkTsb, xTb, h, pA, kh, bias=bcol[0:64, 0:1])
            emit_sqrow(p, kh, sq, pA, knrow, st, scale=1.0)
            for t_ in range(NT):
                mms = []
                for j in range(8):
                    mms.append(lambda e, t_=t_, j=j, h=h: e.matmul(
                        pA[:, 0:64], xTb[:, j * T + t_ * 128:j * T + t_ * 128 + 128],
                        WvTsb[:, j * D + h * 64:j * D + h * 64 + 64],
                        start=(j == 0), stop=False))
                mms.append(lambda e, h=h: e.matmul(
                    pA[:, 0:64], st["onesf"][0:1, 0:128], bvrow[0:1, h * 64:h * 64 + 64],
                    start=False, stop=True))
                p.mmchain(mms)
                p.s(lambda e, t_=t_, vr=vr: e.copy(out=vr[:, t_, 0:64], in_=pA[:, 0:64]))
            hr = slice((h % 2) * 64, (h % 2) * 64 + 64)
            hc = slice((h // 2) * 64, (h // 2) * 64 + 64)
            p.s(lambda e, hr=hr, hc=hc: e.copy(out=lmn2h[:, :], in_=lmn2b[hr, hc]))
            for t_ in range(NT):
                mms = [
                    lambda e, t_=t_: e.matmul(
                        pC[:, 0:64], kh[0:64, t_ * 128:t_ * 128 + 128],
                        lmn2h[0:64, 0:64], start=True, stop=False),
                    lambda e, h=h: e.matmul(
                        pC[:, 0:64], st["onesf"][0:1, 0:128], lnrow[0:1, h * 64:h * 64 + 64],
                        start=False, stop=False),
                    lambda e, t_=t_: e.matmul(
                        pC[:, 0:64], knrow[0:1, t_ * 128:t_ * 128 + 128], st["onesf"][0:1, 0:64],
                        start=False, stop=True),
                ]
                p.mmchain(mms)
                p.v(lambda e: e.tensor_scalar(out=ck[:, :], in0=pC[:, 0:64], scalar1=0.0,
                                              scalar2=None, op0=OP.max))
                p.v(lambda e: e.tensor_scalar(out=ck[:, :], in0=ck[:, :], scalar1=nit[:, 0:1],
                                              scalar2=None, op0=OP.mult))
                p.s(lambda e: e.activation(out=phk[:, :], in_=ck[:, :], func=AF.Exp))
                p.t(lambda e, t_=t_, vr=vr: e.matmul(
                    pZ[0:64, 0:65], phk[:, 0:64], vr[:, t_, :],
                    start=(t_ == 0), stop=(t_ == NT - 1)), inc=1, wait=True)
            p.s(lambda e, h=h: e.copy(
                out=Zs[(h % 2) * 64:(h % 2) * 64 + 64, (h // 2) * 65:(h // 2) * 65 + 65],
                in_=pZ[0:64, 0:65]))
        zr = zo.rearrange("(a p) m -> p a m", p=128)
        for j in range(8):
            p.d(lambda e, j=j: e.dma_start(out=zr[:, j, :], in_=Zs[:, j * 65:j * 65 + 65]))
        p.install(block, sem)
    return nc


def build_L8():
    nc = _mk_nc()
    x = nc.dram_tensor("x", [D, T], F32, kind="ExternalInput").ap()
    WkT = nc.dram_tensor("WkT", [D, D], F32, kind="ExternalInput").ap()
    bk = nc.dram_tensor("bk", [D], F32, kind="ExternalInput").ap()
    WqT = nc.dram_tensor("WqT", [D, D], F32, kind="ExternalInput").ap()
    bq = nc.dram_tensor("bq", [D], F32, kind="ExternalInput").ap()
    WoT = nc.dram_tensor("WoT", [D, D], F32, kind="ExternalInput").ap()
    bo = nc.dram_tensor("bo", [D], F32, kind="ExternalInput").ap()
    temp = nc.dram_tensor("temp", [1, 1], F32, kind="ExternalInput").ap()
    xgown = nc.dram_tensor("xg_own", [4, 64, D], F32, kind="ExternalInput").ap()
    zown = nc.dram_tensor("Z_own", [4, D, 65], F32, kind="ExternalInput").ap()
    cd = decl_consts(nc)
    xo = nc.dram_tensor("x3T", [D, T], F32, kind="ExternalOutput").ap()
    p = Prog()
    with contextlib.ExitStack() as stack:
        sb = lambda n, s, d: stack.enter_context(nc.sbuf_tensor(n, s, d))
        ps_ = lambda n, s, d: stack.enter_context(nc.psum_tensor(n, s, d))
        xT = sb("xT", [128, 8 * T], F32)
        xTb = sb("xTb", [128, 8 * T], BF16)
        Wsb = sb("Wsb", [128, 8 * D], BF16)
        stg = sb("stg", [128, D], F32)
        xg4 = sb("xg4", [64, 512], F32)
        xgf = sb("xgf", [64, D], F32)
        xgTb = sb("xgTb", [128, 8 * 64], BF16)
        lmT = sb("lmT", [128, 8 * 64], F32)
        lmb = sb("lmb", [128, 8 * 64], BF16)
        lmn2b = sb("lmn2b", [128, 8 * 64], BF16)
        lnrow = sb("lnrow", [1, H * 64], F32)
        sq64 = sb("sq64", [64, 64], F32)
        bkT = sb("bkT", [128, 8], F32)
        bqT = sb("bqT", [128, 8], F32)
        boT = sb("boT", [128, 8], F32)
        Z4 = sb("Z4", [128, 8 * 65], F32)
        Zsum = sb("Zsum", [128, 8 * 65], F32)
        qh = sb("qh", [64, T], BF16)
        sq = sb("sq", [64, 512], F32)
        qnrow = sb("qnrow", [1, T], F32)
        nit = sb("nit", [128, 1], F32)
        bcol = sb("bcol", [64, 1], F32)
        lmhb = sb("lmhb", [64, 64], BF16)
        lmn2h = sb("lmn2h", [64, 64], BF16)
        zh = sb("zh", [64, 65], F32)
        ws = sb("ws", [64, 64], F32)
        yns = sb("yns", [64, 64], F32)
        t2 = sb("t2", [64, 64], F32)
        rrow = sb("rrow", [1, 64], F32)
        mb = sb("mb", [64, 1], F32)
        yb = sb("yb", [64, 65], F32)
        ybb = sb("ybb", [64, 65], BF16)
        cq = sb("cq", [64, 512], F32)
        pq = sb("pq", [64, 512], BF16)
        nr = sb("nr", [1, 512], F32)
        rnb = sb("rnb", [64, 512], F32)
        attnT = sb("attnT", [128, 8 * T], BF16)
        tmp8 = sb("tmp8", [8, 128], F32)
        sq2 = sb("sq2", [128, 1024], F32)
        rows = sb("rows", [1, 2048], F32)
        st = alloc_consts(stack, nc, ["eyef", "onesf", "aux64"])
        pA = ps_("pA", [128, 512], F32)
        pB = ps_("pB", [128, 512], F32)
        pC = ps_("pC", [128, 512], F32)
        pD = ps_("pD", [128, 512], F32)
        pE = ps_("pE", [128, 512], F32)
        pO = ps_("pO", [128, 512], F32)
        sem = stack.enter_context(nc.semaphore("S"))
        block = stack.enter_context(nc.Block())
        load_consts(p, st, cd)
        emit_load_xT(p, x, xT, xTb)
        emit_bT(p, bk, tmp8, pA, bkT, st)
        emit_bT(p, bq, tmp8, pA, bqT, st)
        emit_bT(p, bo, tmp8, pA, boT, st)
        p.d(lambda e: e.dma_start(out=nit[0:1, 0:1], in_=temp))
        p.v(lambda e: e.reciprocal(nit[0:1, 0:1], nit[0:1, 0:1]))
        p.t(lambda e: e.matmul(pA[:, 0:1], st["onesf"][0:1, 0:128], nit[0:1, 0:1],
                               start=True, stop=True))
        p.s(lambda e: e.activation(out=nit[:, :], in_=pA[:, 0:1], func=AF.Copy, scale=-1.0))
        emit_wload(p, WkT, stg, Wsb)
        emit_xg_lm(p, st, xgown, Wsb, bkT, xg4, xgf, xgTb, lmT, lmb, lmn2b, lnrow, sq64, pA, pE)
        for i in range(4):
            zr = zown.rearrange("c (a p) m -> c p a m", p=128)
            p.d(lambda e, i=i, zr=zr: e.dma_start(
                out=Z4[:, :].rearrange("p (a m) -> p a m", m=65), in_=zr[i, :, :, :]))
            if i == 0:
                p.v(lambda e: e.tensor_copy(out=Zsum[:, :], in_=Z4[:, :]))
            else:
                p.v(lambda e: e.tensor_tensor(out=Zsum[:, :], in0=Zsum[:, :], in1=Z4[:, :], op=OP.add))
        emit_wload(p, WqT, stg, Wsb)
        for h in range(H):
            hr = slice((h % 2) * 64, (h % 2) * 64 + 64)
            hc64 = slice((h // 2) * 64, (h // 2) * 64 + 64)
            hc65 = slice((h // 2) * 65, (h // 2) * 65 + 65)
            p.s(lambda e, hr=hr, hc64=hc64: e.copy(out=lmhb[:, :], in_=lmb[hr, hc64]))
            p.s(lambda e, hr=hr, hc64=hc64: e.copy(out=lmn2h[:, :], in_=lmn2b[hr, hc64]))
            p.s(lambda e, hr=hr, hc65=hc65: e.copy(out=zh[:, :], in_=Zsum[hr, hc65]))
            mms = [
                lambda e: e.matmul(pC[0:64, 0:64], lmhb[0:64, 0:64],
                                   lmn2h[0:64, 0:64], start=True, stop=False),
                lambda e, h=h: e.matmul(pC[0:64, 0:64], st["onesf"][0:1, 0:64],
                                        lnrow[0:1, h * 64:h * 64 + 64], start=False, stop=False),
                lambda e, h=h: e.matmul(pC[0:64, 0:64], lnrow[0:1, h * 64:h * 64 + 64],
                                        st["onesf"][0:1, 0:64], start=False, stop=True),
            ]
            p.mmchain(mms)
            p.v(lambda e: e.tensor_scalar(out=ws[:, :], in0=pC[0:64, 0:64], scalar1=0.0,
                                          scalar2=None, op0=OP.max))
            p.v(lambda e: e.tensor_scalar(out=ws[:, :], in0=ws[:, :], scalar1=nit[0:64, 0:1],
                                          scalar2=None, op0=OP.mult))
            p.s(lambda e: e.activation(out=ws[:, :], in_=ws[:, :], func=AF.Exp))
            p.v(lambda e: e.tensor_tensor(out=ws[:, :], in0=ws[:, :], in1=st["aux64"][0:64, 0:64],
                                          op=OP.add))
            p.v(lambda e: e.reduce_sum(mb[:, 0:1], ws[:, :], axis=AX.X))
            p.t(lambda e: e.transpose(pE[0:1, 0:64], mb[0:64, 0:1], st["eyef"][0:64, 0:64]))
            p.s(lambda e: e.copy(out=rrow[:, :], in_=pE[0:1, 0:64]))
            p.v(lambda e: e.reduce_max(rrow[0:1, 0:1], rrow[:, :], axis=AX.X))
            p.v(lambda e: e.reciprocal(rrow[0:1, 0:1], rrow[0:1, 0:1]))
            p.v(lambda e: e.tensor_tensor(out=rrow[0:1, 0:1], in0=rrow[0:1, 0:1],
                                          in1=rrow[0:1, 0:1], op=OP.mult))
            p.t(lambda e: e.matmul(pE[0:64, 0:1], st["onesf"][0:1, 0:64], rrow[0:1, 0:1],
                                   start=True, stop=True))
            p.s(lambda e: e.copy(out=mb[:, :], in_=pE[0:64, 0:1]))
            p.v(lambda e: e.tensor_scalar(out=yns[:, :], in0=ws[:, :], scalar1=mb[:, 0:1],
                                          scalar2=None, op0=OP.mult))
            for it in range(22):
                p.t(lambda e: e.matmul(pD[0:64, 0:64], ws[0:64, 0:64], yns[0:64, 0:64],
                                       start=True, stop=True))
                p.s(lambda e: e.activation(out=t2[:, :], in_=pD[0:64, 0:64], func=AF.Copy,
                                           scale=-1.0))
                p.v(lambda e: e.tensor_tensor(out=t2[:, :], in0=t2[:, :],
                                              in1=st["aux64"][0:64, 64:128], op=OP.add))
                p.t(lambda e: e.matmul(pD[0:64, 64:128], yns[0:64, 0:64], t2[0:64, 0:64],
                                       start=True, stop=True))
                p.s(lambda e: e.copy(out=yns[:, :], in_=pD[0:64, 64:128]))
            p.t(lambda e: e.matmul(pD[0:64, 0:65], yns[0:64, 0:64],
                                   zh[0:64, 0:65], start=True, stop=True))
            p.s(lambda e: e.copy(out=yb[:, :], in_=pD[0:64, 0:65]))
            p.s(lambda e: e.copy(out=ybb[:, :], in_=yb[:, :]))
            p.s(lambda e, h=h: e.copy(out=bcol[0:64, 0:1],
                                      in_=bqT[(h % 2) * 64:(h % 2) * 64 + 64, (h // 2):(h // 2) + 1]))
            emit_headprojT(p, Wsb, xTb, h, pA, qh, bias=bcol[0:64, 0:1])
            emit_sqrow(p, qh, sq, pA, qnrow, st, scale=1.0)
            for tc in range(NC4):
                mms = [
                    lambda e, tc=tc: e.matmul(
                        pO[0:64, 0:512], lmn2h[0:64, 0:64], qh[0:64, tc * 512:tc * 512 + 512],
                        start=True, stop=False),
                    lambda e, tc=tc: e.matmul(
                        pO[0:64, 0:512], st["onesf"][0:1, 0:64],
                        qnrow[0:1, tc * 512:tc * 512 + 512], start=False, stop=False),
                    lambda e, h=h: e.matmul(
                        pO[0:64, 0:512], lnrow[0:1, h * 64:h * 64 + 64], st["onesf"][0:1, 0:512],
                        start=False, stop=True),
                ]
                p.mmchain(mms)
                p.v(lambda e: e.tensor_scalar(out=cq[:, :], in0=pO[0:64, 0:512], scalar1=0.0,
                                              scalar2=None, op0=OP.max))
                p.v(lambda e: e.tensor_scalar(out=cq[:, :], in0=cq[:, :], scalar1=nit[0:64, 0:1],
                                              scalar2=None, op0=OP.mult))
                p.s(lambda e: e.activation(out=pq[:, :], in_=cq[:, :], func=AF.Exp))
                p.t(lambda e: e.matmul(pO[0:65, 0:512], ybb[0:64, 0:65], pq[0:64, 0:512],
                                       start=True, stop=True))
                p.v(lambda e: e.tensor_scalar(out=nr[:, :], in0=pO[64:65, 0:512], scalar1=1e-10,
                                              scalar2=None, op0=OP.max))
                p.v(lambda e: e.reciprocal(nr[:, :], nr[:, :]))
                p.t(lambda e: e.matmul(pB[0:64, 0:512], st["onesf"][0:1, 0:64], nr[0:1, 0:512],
                                       start=True, stop=True))
                p.s(lambda e: e.copy(out=rnb[:, :], in_=pB[0:64, 0:512]))
                p.v(lambda e, h=h, tc=tc: e.tensor_tensor(
                    out=attnT[(h % 2) * 64:(h % 2) * 64 + 64,
                              (h // 2) * T + tc * 512:(h // 2) * T + tc * 512 + 512],
                    in0=pO[0:64, 0:512], in1=rnb[:, :], op=OP.mult))
        emit_wload(p, WoT, stg, Wsb)
        for ob in range(8):
            for tc in range(NC4):
                mms = []
                for j in range(8):
                    mms.append(lambda e, ob=ob, tc=tc, j=j: e.matmul(
                        pA[:, 0:512], Wsb[:, j * D + ob * 128:j * D + ob * 128 + 128],
                        attnT[:, j * T + tc * 512:j * T + tc * 512 + 512],
                        start=(j == 0), stop=(j == 7)))
                p.mmchain(mms)
                p.v(lambda e, ob=ob: e.tensor_scalar(
                    out=sq2[:, 0:512], in0=pA[:, 0:512], scalar1=boT[:, ob:ob + 1],
                    scalar2=None, op0=OP.add))
                p.v(lambda e, ob=ob, tc=tc: e.tensor_tensor(
                    out=xT[:, ob * T + tc * 512:ob * T + tc * 512 + 512],
                    in0=xT[:, ob * T + tc * 512:ob * T + tc * 512 + 512],
                    in1=sq2[:, 0:512], op=OP.add))
        emit_ln_inplace(p, xT, None, pB, pC, pD, sq2, rows, st)
        emit_store_xT(p, xo, xT)
        p.install(block, sem)
    return nc


def build_L9():
    nc = _mk_nc()
    x = nc.dram_tensor("x", [D, T], F32, kind="ExternalInput").ap()
    gwT = nc.dram_tensor("gwT", [D, 8], F32, kind="ExternalInput").ap()
    w13T = nc.dram_tensor("w13T", [8, D, 4096], F32, kind="ExternalInput").ap()
    w2T = nc.dram_tensor("w2T", [8, 2048, D], F32, kind="ExternalInput").ap()
    cd = decl_consts(nc)
    yo = nc.dram_tensor("yT", [D, T], F32, kind="ExternalOutput").ap()
    p = Prog()
    TH = 1024
    with contextlib.ExitStack() as stack:
        sb = lambda n, s, d: stack.enter_context(nc.sbuf_tensor(n, s, d))
        ps_ = lambda n, s, d: stack.enter_context(nc.psum_tensor(n, s, d))
        xTb = sb("xTb", [128, 8 * T], BF16)
        chunk = sb("chunk", [128, T], F32)
        stg = sb("stg", [128, D], F32)
        wsb = sb("wsb", [128, 8 * 4096], BF16)
        actT = sb("actT", [128, 16 * TH], BF16)
        outacc = sb("outacc", [128, 8 * TH], F32)
        gsb = sb("gsb", [128, 64], F32)
        lg = sb("lg", [8, T], F32)
        prb = sb("prb", [128, NT * 8], F32)
        wtf = sb("wtf", [128, NT * 8], F32)
        eq8 = sb("eq8", [128, 8], F32)
        mk8 = sb("mk8", [128, 8], F32)
        g4 = sb("g4", [128, 4], F32)
        g4b = sb("g4b", [128, 4], F32)
        sm8 = sb("sm8", [128, 8], F32)
        m1c = sb("m1c", [128, 1], F32)
        m2c = sb("m2c", [128, 1], F32)
        crow = sb("crow", [1, 512], F32)
        cb = sb("cb", [128, 512], F32)
        silt = sb("silt", [128, 512], F32)
        actf = sb("actf", [128, 512], F32)
        rows = sb("rows", [1, 2048], F32)
        sq2 = sb("sq2", [128, 1024], F32)
        st = alloc_consts(stack, nc, ["eyef", "onesf"])
        pG = [ps_("pG%d" % i, [8, 512], F32) for i in range(4)]
        pU = ps_("pU", [128, 512], F32)
        pV = ps_("pV", [128, 512], F32)
        pW = ps_("pW", [128, 512], F32)
        pX = ps_("pX", [128, 512], F32)
        sem = stack.enter_context(nc.semaphore("S"))
        block = stack.enter_context(nc.Block())
        load_consts(p, st, cd)
        xr = x.rearrange("(a p) m -> p a m", p=128)
        gr = gwT.rearrange("(a p) m -> p a m", p=128)
        for j in range(8):
            p.d(lambda e, j=j: e.dma_start(out=gsb[:, j * 8:j * 8 + 8], in_=gr[:, j, :]))
        for j in range(8):
            p.d(lambda e, j=j: e.dma_start(out=chunk[:, :], in_=xr[:, j, :]))
            p.s(lambda e, j=j: e.copy(out=xTb[:, j * T:(j + 1) * T], in_=chunk[:, :]))
            for tc in range(NC4):
                p.t(lambda e, j=j, tc=tc: e.matmul(
                    pG[tc][0:8, 0:512], gsb[:, j * 8:j * 8 + 8], chunk[:, tc * 512:tc * 512 + 512],
                    start=(j == 0), stop=(j == 7)), inc=1, wait=True)
        for tc in range(NC4):
            p.s(lambda e, tc=tc: e.copy(out=lg[:, tc * 512:tc * 512 + 512], in_=pG[tc][0:8, 0:512]))
        for t_ in range(NT):
            p.t(lambda e, t_=t_: e.transpose(pU[0:128, 0:8], lg[0:8, t_ * 128:t_ * 128 + 128],
                                             st["eyef"][0:8, 0:8]))
            p.s(lambda e, t_=t_: e.copy(out=prb[:, t_ * 8:t_ * 8 + 8], in_=pU[0:128, 0:8]))
            pr = prb[:, t_ * 8:t_ * 8 + 8]
            p.v(lambda e, pr=pr: e.reduce_max(m1c[:, 0:1], pr, axis=AX.X, negate=True))
            p.v(lambda e, pr=pr: e.tensor_scalar(out=pr, in0=pr, scalar1=m1c[:, 0:1],
                                                 scalar2=None, op0=OP.add))
            p.s(lambda e, pr=pr: e.activation(out=pr, in_=pr, func=AF.Exp))
            p.v(lambda e, pr=pr: e.reduce_sum(m1c[:, 0:1], pr, axis=AX.X))
            p.v(lambda e: e.reciprocal(m1c[:, 0:1], m1c[:, 0:1]))
            p.v(lambda e, pr=pr: e.tensor_scalar(out=pr, in0=pr, scalar1=m1c[:, 0:1],
                                                 scalar2=None, op0=OP.mult))
            prr = pr.rearrange("p (g two) -> p g two", two=2)
            p.v(lambda e, prr=prr: e.tensor_tensor(out=g4[:, :], in0=prr[:, :, 0],
                                                   in1=prr[:, :, 1], op=OP.max))
            p.v(lambda e: e.reduce_max(m1c[:, 0:1], g4[:, :], axis=AX.X))
            p.v(lambda e: e.tensor_scalar(out=eq8[:, 0:4], in0=g4[:, :], scalar1=m1c[:, 0:1],
                                          scalar2=None, op0=OP.is_equal))
            p.v(lambda e: e.scalar_tensor_tensor(out=g4b[:, :], in0=eq8[:, 0:4], scalar=-2.0,
                                                 in1=g4[:, :], op0=OP.mult, op1=OP.add))
            p.v(lambda e: e.reduce_max(m2c[:, 0:1], g4b[:, :], axis=AX.X))
            p.v(lambda e: e.tensor_scalar(out=g4[:, :], in0=g4[:, :], scalar1=m2c[:, 0:1],
                                          scalar2=None, op0=OP.is_ge))
            smr = sm8[:, :].rearrange("p (g two) -> p g two", two=2)
            p.v(lambda e, smr=smr: e.tensor_copy(out=smr[:, :, 0], in_=g4[:, :]))
            p.v(lambda e, smr=smr: e.tensor_copy(out=smr[:, :, 1], in_=g4[:, :]))
            p.v(lambda e, pr=pr: e.tensor_tensor(out=sm8[:, :], in0=sm8[:, :], in1=pr, op=OP.mult))
            p.v(lambda e: e.reduce_max(m1c[:, 0:1], sm8[:, :], axis=AX.X))
            p.v(lambda e: e.tensor_scalar(out=eq8[:, :], in0=sm8[:, :], scalar1=m1c[:, 0:1],
                                          scalar2=None, op0=OP.is_equal))
            p.v(lambda e: e.scalar_tensor_tensor(out=mk8[:, :], in0=eq8[:, :], scalar=-2.0,
                                                 in1=sm8[:, :], op0=OP.mult, op1=OP.add))
            p.v(lambda e: e.reduce_max(m2c[:, 0:1], mk8[:, :], axis=AX.X))
            p.v(lambda e: e.tensor_scalar(out=eq8[:, :], in0=sm8[:, :], scalar1=m2c[:, 0:1],
                                          scalar2=None, op0=OP.is_ge))
            p.v(lambda e: e.tensor_tensor(out=mk8[:, :], in0=sm8[:, :], in1=eq8[:, :], op=OP.mult))
            p.v(lambda e: e.tensor_tensor(out=m1c[:, 0:1], in0=m1c[:, 0:1], in1=m2c[:, 0:1],
                                          op=OP.add))
            p.v(lambda e: e.reciprocal(m1c[:, 0:1], m1c[:, 0:1]))
            p.v(lambda e, t_=t_: e.tensor_scalar(out=wtf[:, t_ * 8:t_ * 8 + 8], in0=mk8[:, :],
                                                 scalar1=m1c[:, 0:1], scalar2=None, op0=OP.mult))
        for half in range(2):
            for e_ in range(8):
                wr = w13T.rearrange("c (a p) m -> c p a m", p=128)
                for j in range(8):
                    for q in range(4):
                        p.d(lambda e, j=j, q=q, wr=wr, e_=e_: e.dma_start(
                            out=stg[:, :], in_=wr[e_, :, j, q * 1024:q * 1024 + 1024]))
                        p.s(lambda e, j=j, q=q: e.copy(
                            out=wsb[:, j * 4096 + q * 1024:j * 4096 + q * 1024 + 1024],
                            in_=stg[:, :]))
                for tc in range(2):
                    for q in range(4):
                        p.t(lambda e, half=half, tc=tc, q=q, e_=e_: e.transpose(
                            pU[0:1, q * 128:q * 128 + 128],
                            wtf[:, (half * 8 + tc * 4 + q) * 8 + e_:(half * 8 + tc * 4 + q) * 8 + e_ + 1],
                            st["eyef"][:, :]),
                            inc=(1 if q == 3 else 0), wait=(q == 0))
                    p.s(lambda e: e.copy(out=crow[:, :], in_=pU[0:1, 0:512]))
                    p.t(lambda e: e.matmul(pU[:, 0:512], st["onesf"][0:1, 0:128], crow[0:1, 0:512],
                                           start=True, stop=True))
                    p.s(lambda e, tc=tc: e.copy(out=cb[:, :] if tc == 0 else sq2[:, 0:512],
                                                in_=pU[:, 0:512]))
                for ib in range(16):
                    for tc in range(2):
                        mmsg = []
                        mmsu = []
                        for j in range(8):
                            mmsg.append(lambda e, ib=ib, tc=tc, j=j, half=half: e.matmul(
                                pV[:, 0:512], wsb[:, j * 4096 + ib * 128:j * 4096 + ib * 128 + 128],
                                xTb[:, j * T + half * TH + tc * 512:j * T + half * TH + tc * 512 + 512],
                                start=(j == 0), stop=(j == 7)))
                            mmsu.append(lambda e, ib=ib, tc=tc, j=j, half=half: e.matmul(
                                pW[:, 0:512],
                                wsb[:, j * 4096 + 2048 + ib * 128:j * 4096 + 2048 + ib * 128 + 128],
                                xTb[:, j * T + half * TH + tc * 512:j * T + half * TH + tc * 512 + 512],
                                start=(j == 0), stop=(j == 7)))
                        p.mmchain(mmsg)
                        p.mmchain(mmsu)
                        p.s(lambda e: e.activation(out=silt[:, :], in_=pV[:, 0:512], func=AF.Silu))
                        p.v(lambda e: e.tensor_tensor(out=actf[:, :], in0=silt[:, :],
                                                      in1=pW[:, 0:512], op=OP.mult))
                        p.v(lambda e, ib=ib, tc=tc: e.tensor_tensor(
                            out=actT[:, ib * TH + tc * 512:ib * TH + tc * 512 + 512],
                            in0=actf[:, :], in1=(cb[:, :] if tc == 0 else sq2[:, 0:512]),
                            op=OP.mult))
                w2r = w2T.rearrange("c (a p) m -> c p a m", p=128)
                for j in range(16):
                    p.d(lambda e, j=j, w2r=w2r, e_=e_: e.dma_start(
                        out=stg[:, :], in_=w2r[e_, :, j, :]))
                    p.s(lambda e, j=j: e.copy(out=wsb[:, j * 1024:j * 1024 + 1024], in_=stg[:, :]))
                for ob in range(8):
                    for tc in range(2):
                        mms = []
                        for j in range(16):
                            mms.append(lambda e, ob=ob, tc=tc, j=j: e.matmul(
                                pX[:, 0:512], wsb[:, j * 1024 + ob * 128:j * 1024 + ob * 128 + 128],
                                actT[:, j * TH + tc * 512:j * TH + tc * 512 + 512],
                                start=(j == 0), stop=(j == 15)))
                        p.mmchain(mms)
                        sl = slice(ob * TH + tc * 512, ob * TH + tc * 512 + 512)
                        if e_ == 0:
                            p.v(lambda e, sl=sl: e.tensor_copy(out=outacc[:, sl], in_=pX[:, 0:512]))
                        else:
                            p.v(lambda e, sl=sl: e.tensor_tensor(out=outacc[:, sl],
                                                                 in0=outacc[:, sl],
                                                                 in1=pX[:, 0:512], op=OP.add))
            for j in range(8):
                p.d(lambda e, j=j, half=half: e.dma_start(
                    out=chunk[:, 0:TH], in_=xr[:, j, half * TH:half * TH + TH]))
                for tc in range(2):
                    sl = slice(j * TH + tc * 512, j * TH + tc * 512 + 512)
                    p.v(lambda e, sl=sl, tc=tc: e.tensor_tensor(
                        out=outacc[:, sl], in0=outacc[:, sl],
                        in1=chunk[:, tc * 512:tc * 512 + 512], op=OP.add))
            emit_ln_inplace(p, outacc, None, pV, pW, pX, sq2, rows, st, Tw=TH, ncw=2)
            yr = yo.rearrange("(a p) m -> p a m", p=128)
            for j in range(8):
                p.d(lambda e, j=j, half=half, yr=yr: e.dma_start(
                    out=yr[:, j, half * TH:half * TH + TH], in_=outacc[:, j * TH:j * TH + TH]))
        p.install(block, sem)
    return nc



KERNEL_BUILD_CACHE = {}


def _run(key, build, in_maps):
    nc = KERNEL_BUILD_CACHE.get(key)
    if nc is None:
        nc = build()
        KERNEL_BUILD_CACHE[key] = nc
    res = run_bass_kernel_spmd(nc, in_maps, core_ids=list(range(C)))
    return res.results


def kernel(**inp):
    f = lambda k: np.ascontiguousarray(inp[k].astype(np.float32, copy=False))
    x = f("L0_attn_x").reshape(C * T, D)
    xT_sh = [np.ascontiguousarray(x[c * T:(c + 1) * T].T) for c in range(C)]
    csts = [consts_for_core(c) for c in range(C)]
    PT = np.ascontiguousarray(f("L0_attn_P").T)

    im = [dict(x=xT_sh[c], Wk=f("L0_attn_Wk"), PT=PT, **csts[c]) for c in range(C)]
    r = _run("L1", build_L1, im)
    mx_all = np.concatenate([np.asarray(r[c]["mx_part"]).reshape(1) for c in range(C)])
    DEBUG["mx_all"] = mx_all
    if _stop_after == "L1":
        return None

    im = [dict(x=xT_sh[c], Wk=f("L0_attn_Wk"), Wv=f("L0_attn_Wv"), PT=PT,
               mx_all=mx_all, **csts[c]) for c in range(C)]
    r = _run("L2", build_L2, im)
    ctx_all = np.stack([np.asarray(r[c]["ctx_part"]) for c in range(C)])
    DEBUG["ctx_all"] = ctx_all
    if _stop_after == "L2":
        return None

    im = [dict(x=xT_sh[c], Wq=f("L0_attn_Wq"), Wo=f("L0_attn_Wo"), bo=f("L0_attn_bo"),
               PT=PT, ctx_own=ctx_all[(c // 4) * 4:(c // 4) * 4 + 4],
               gw=f("L0_ffn_gate_w"), gb=f("L0_ffn_gate_b"), **csts[c]) for c in range(C)]
    r = _run("L3", build_L3, im)
    x1T = [np.ascontiguousarray(np.asarray(r[c]["x1T"])) for c in range(C)]
    stats = np.stack([np.asarray(r[c]["stats"]).reshape(16) for c in range(C)])
    DEBUG["x1T"] = x1T
    DEBUG["stats"] = stats
    if _stop_after == "L3":
        return None

    im = [dict(x=x1T[c], gw=f("L0_ffn_gate_w"), gb=f("L0_ffn_gate_b"),
               stats_own=stats[(c // 4) * 4:(c // 4) * 4 + 4],
               w1=f("L0_ffn_w1"), b1=f("L0_ffn_b1"), w2=f("L0_ffn_w2"), b2=f("L0_ffn_b2"),
               w3=f("L0_ffn_w3"), b3=f("L0_ffn_b3"), **csts[c]) for c in range(C)]
    r = _run("L4", build_L4, im)
    v_all = np.stack([np.asarray(r[c]["v_part"]).reshape(8192) for c in range(C)])
    DEBUG["v_all"] = v_all
    if _stop_after == "L4":
        return None

    im = [dict(x=x1T[c], gw=f("L0_ffn_gate_w"), gb=f("L0_ffn_gate_b"),
               stats_own=stats[(c // 4) * 4:(c // 4) * 4 + 4],
               v_own=v_all, **csts[c]) for c in range(C)]
    r = _run("L5", build_L5, im)
    x2T = [np.ascontiguousarray(np.asarray(r[c]["x2T"])) for c in range(C)]
    DEBUG["x2T"] = x2T
    if _stop_after == "L5":
        return None

    im = [dict(x=x2T[c], lidx=np.ascontiguousarray(inp["L1_attn_landmark_idx"].astype(np.int32)),
               **csts[c]) for c in range(C)]
    r = _run("L6", build_L6, im)
    xg_all = np.stack([np.asarray(r[c]["xg_part"]) for c in range(C)])
    DEBUG["xg_all"] = xg_all
    if _stop_after == "L6":
        return None

    tW = lambda k: np.ascontiguousarray(f(k).T)
    tscal = np.ascontiguousarray(inp["L1_attn_temperature"].astype(np.float32).reshape(1, 1))
    im = [dict(x=x2T[c], WkT=tW("L1_attn_Wk"), bk=f("L1_attn_bk"),
               WvT=tW("L1_attn_Wv"), bv=f("L1_attn_bv"), temp=tscal,
               xg_own=xg_all[(c // 4) * 4:(c // 4) * 4 + 4], **csts[c]) for c in range(C)]
    r = _run("L7", build_L7, im)
    Z_all = np.stack([np.asarray(r[c]["Z_part"]) for c in range(C)])
    DEBUG["Z_all"] = Z_all
    if _stop_after == "L7":
        return None

    im = [dict(x=x2T[c], WkT=tW("L1_attn_Wk"), bk=f("L1_attn_bk"),
               WqT=tW("L1_attn_Wq"), bq=f("L1_attn_bq"),
               WoT=tW("L1_attn_Wo"), bo=f("L1_attn_bo"), temp=tscal,
               xg_own=xg_all[(c // 4) * 4:(c // 4) * 4 + 4],
               Z_own=Z_all[(c // 4) * 4:(c // 4) * 4 + 4], **csts[c]) for c in range(C)]
    r = _run("L8", build_L8, im)
    x3T = [np.ascontiguousarray(np.asarray(r[c]["x3T"])) for c in range(C)]
    DEBUG["x3T"] = x3T
    if _stop_after == "L8":
        return None

    w13t = np.ascontiguousarray(np.transpose(f("L1_ffn_w13_weight"), (0, 2, 1)))
    w2t = np.ascontiguousarray(np.transpose(f("L1_ffn_w2_weight"), (0, 2, 1)))
    im = [dict(x=x3T[c], gwT=np.ascontiguousarray(f("L1_ffn_gate_w").T),
               w13T=w13t, w2T=w2t, **csts[c]) for c in range(C)]
    r = _run("L9", build_L9, im)
    yT = [np.asarray(r[c]["yT"]) for c in range(C)]
    out = np.stack([y.T for y in yT]).reshape(2, 8192, 1024).astype(np.float32)
    return out
